# version 06 (final)
# baseline (speedup 1.0000x reference)
import numpy as np
import ml_dtypes
from contextlib import ExitStack

import concourse.bass as bass
import concourse.mybir as mybir
from concourse.bass_utils import run_bass_kernel_spmd

F32 = mybir.dt.float32
BF16 = mybir.dt.bfloat16
AF = mybir.ActivationFunctionType
ALU = mybir.AluOpType
AX = mybir.AxisListType
BF = ml_dtypes.bfloat16

B, S, D, H, DH, DIM = 4, 4096, 2048, 16, 128, 64
QF = H * DIM
NCHK = D // 128
TT_ALL = 32
TT_OWN = 16
EPS_LN = 1e-5
EPS_Q = 1e-5
EPS_A = 1e-6

SEM_OF = {"sync": "ld", "gpsimd": "g", "scalar": "a", "vector": "v", "tensor": "t"}

_DIRECT_DMA_WAIT = {("vector", "ld"), ("vector", "g"),
                    ("scalar", "ld"), ("sync", "ld")}


class Cx:
    def __init__(self, mode, sems, eng=None, evt=None):
        self.mode = mode
        self.sems = sems
        self.eng = eng
        self.n = {"ld": 0, "g": 0, "a": 0, "v": 0, "t": 0, "r": 0}
        self.relayed = {}
        self.planning = evt is None
        self.evt = {} if evt is None else evt

    def _rec(self, key, sem):
        if key is None:
            return
        if self.planning:
            assert key not in self.evt, f"dup event {key}"
            self.evt[key] = (sem, self.n[sem])
        else:
            assert self.evt[key] == (sem, self.n[sem]), \
                f"evt mismatch {key}: {self.evt[key]} vs {(sem, self.n[sem])}"

    def dma(self, engname, fn, key=None):
        sem = SEM_OF[engname]
        if self.mode == engname:
            fn(self.eng).then_inc(self.sems[sem], 16)
        self.n[sem] += 16
        self._rec(key, sem)

    def op(self, engname, fn, key=None):
        sem = SEM_OF[engname]
        if self.mode == engname:
            inst = fn(self.eng)
            if key is not None:
                inst.then_inc(self.sems[sem], 1)
        if key is not None:
            self.n[sem] += 1
            self._rec(key, sem)

    def ops(self, engname, fn, key=None):
        sem = SEM_OF[engname]
        if self.mode == engname:
            last = fn(self.eng)
            if key is not None:
                assert last is not None, f"ops({key}): fn returned None"
                last.then_inc(self.sems[sem], 1)
        if key is not None:
            self.n[sem] += 1
            self._rec(key, sem)

    def wait(self, engname, key):
        sem, val = self.evt[key]
        if sem not in ("ld", "g") or (engname, sem) in _DIRECT_DMA_WAIT:
            if self.mode == engname:
                self.eng.wait_ge(self.sems[sem], val)
            return
        rv = self.relayed.get(key)
        if rv is None:
            self.n["r"] += 1
            rv = self.n["r"]
            self.relayed[key] = rv
            if self.mode == "vector":
                self.eng.wait_ge(self.sems[sem], val)
                self.eng.sem_inc(self.sems["r"], 1)
        if self.mode == engname:
            self.eng.wait_ge(self.sems["r"], rv)


class Ring:
    def __init__(self, slots, qname):
        self.slots = slots
        self.q = qname
        self.hist = []

    def load(self, cx, consumer_key, load_key, fn_of_slot, pre_waits=()):
        for w in pre_waits:
            cx.wait(self.q, w)
        i = len(self.hist)
        slot = self.slots[i % len(self.slots)]
        if i >= len(self.slots):
            prev = self.hist[i - len(self.slots)]
            if prev is not None:
                cx.wait(self.q, prev)
        self.hist.append(consumer_key)
        cx.dma(self.q, lambda e: fn_of_slot(e, slot), key=load_key)
        return slot


def build_nc(stop_after=None):
    _stops = ["p0", "a1q", "a1", "ek", "wv", "a2", "eqwobd", "b1", "c"]
    _cut = _stops.index(stop_after) if stop_after is not None else len(_stops)

    def _stop(name):
        return _stops.index(name) >= _cut

    nc = bass.Bass("TRN2", target_bir_lowering=False,
                   detect_race_conditions=False)

    dr = {}
    def din(name, shape, dt=F32):
        dr[name] = nc.dram_tensor(name, list(shape), dt, kind="ExternalInput").ap()
    din("x", (S, D))
    din("wffnT", (D, D))
    din("wq", (D, D)); din("wk", (D, D))
    din("wvT", (D, D)); din("woT", (D, D))
    din("wmapqT", (DH, DIM)); din("wmapkT", (DH, DIM))
    din("bq2", (128, H)); din("bk2", (128, H))
    din("brows", (1, 3 * D))
    din("bfb_bc", (128, D))
    din("eye", (128, 128), BF16)
    din("e16", (H, 8 * 128), BF16)
    din("ones1x128b", (1, 128), BF16)
    din("onesrowf", (1, 128))
    din("onescolf", (128, 1))
    din("onescolb", (128, 1), BF16)
    din("ones512b", (1, 512), BF16)
    y_out = nc.dram_tensor("y", [TT_OWN * 128, D], F32, kind="ExternalOutput").ap()
    x1s = nc.dram_tensor("x1s", [TT_ALL * 128, D], F32, kind="Internal").ap()
    x2s = nc.dram_tensor("x2s", [TT_OWN * 128, D], F32, kind="Internal").ap()
    wqts = nc.dram_tensor("wqts", [128, NCHK * D], BF16, kind="Internal").ap()
    ln2s = nc.dram_tensor("ln2s", [128, 4 * NCHK * 512], BF16, kind="Internal").ap()

    es_all = ExitStack()

    def sbt(name, shape, dt, st=None):
        return (st or es_all).enter_context(nc.sbuf_tensor(name, list(shape), dt))

    def pst(name, shape, dt, st=None):
        return (st or es_all).enter_context(nc.psum_tensor(name, list(shape), dt))

    T = {}
    T["wmapqT"] = sbt("s_wmapqT", (DH, DIM), BF16)
    T["wmapkT"] = sbt("s_wmapkT", (DH, DIM), BF16)
    T["bq2"] = sbt("s_bq2", (128, H), BF16)
    T["bk2"] = sbt("s_bk2", (128, H), BF16)
    T["eye"] = sbt("s_eye", (128, 128), BF16)
    T["e16"] = sbt("s_e16", (H, 8 * 128), BF16)
    T["o1x128b"] = sbt("s_o1x128b", (1, 128), BF16)
    T["orowf"] = sbt("s_orowf", (1, 128), F32)
    T["ocolf"] = sbt("s_ocolf", (128, 1), F32)
    T["ocolb"] = sbt("s_ocolb", (128, 1), BF16)
    T["o512b"] = sbt("s_o512b", (1, 512), BF16)
    T["rows16"] = sbt("s_rows16", (16, 128), BF16)
    T["rowsb"] = sbt("s_rowsb", (1, 2 * D + 2 * QF), BF16)
    T["col"] = sbt("s_col", (128, 16), F32)
    T["wsum"] = sbt("s_wsum", (128, 18), F32)
    T["whl"] = sbt("s_whl", (128, 2), BF16)
    T["ahl"] = sbt("s_ahl", (1, 2), BF16)
    T["acol"] = sbt("s_acol", (128, 4), F32)
    T["kv"] = sbt("s_kv", (128, 3 * 512), BF16)
    T["ringx"] = [sbt(f"s_ringx{i}", (128, D), F32) for i in range(2)]
    T["outb"] = sbt("s_outb", (128, D), F32)
    T["vdum"] = sbt("s_vdum", (128, 1), F32)
    T["alab"] = sbt("s_alab", (1, 1), F32)

    es_pa = ExitStack()
    T["eff"] = sbt("s_eff", (128, NCHK * QF), BF16, es_pa)
    es_rw = ExitStack()
    T["ringw"] = [sbt(f"s_ringw{i}", (128, D), F32, es_rw) for i in range(2)]
    T["ringwb"] = [sbt(f"s_ringwb{i}", (128, D), BF16, es_rw) for i in range(2)]
    es_a1 = ExitStack()
    T["wqt"] = sbt("s_wqt", (128, NCHK * D), BF16, es_a1)
    T["lnf"] = sbt("s_lnf", (128, D), F32, es_a1)
    T["frac"] = sbt("s_frac", (128, D), F32, es_a1)
    T["t1"] = sbt("s_t1", (128, D), F32, es_a1)
    T["bfbh"] = sbt("s_bfbh", (128, D), F32, es_a1)
    T["qxtm"] = sbt("s_qxtm", (128, D), BF16, es_a1)
    T["qxfm"] = sbt("s_qxfm", (128, D), BF16, es_a1)
    es_a1.close()

    es_a2 = ExitStack()
    T["wvt"] = sbt("s_wvt", (128, NCHK * D), BF16, es_a2)
    T["ke"] = [sbt(f"s_ke{i}", (128, QF), BF16, es_a2) for i in range(2)]
    T["vt"] = [sbt(f"s_vt{i}", (128, H * 132), BF16, es_a2) for i in range(2)]
    T["ascr"] = sbt("s_ascr", (128, QF), F32, es_a2)
    T["escr"] = sbt("s_escr", (128, QF), F32, es_a2)
    T["qxtm2"] = sbt("s_qxtm2", (128, D), BF16, es_a2)
    T["qxfm2"] = sbt("s_qxfm2", (128, D), BF16, es_a2)
    es_a2.close()

    es_rw.close()
    es_b1 = ExitStack()
    T["wot"] = sbt("s_wot", (128, NCHK * D), BF16, es_b1)
    T["qe"] = sbt("s_qe", (128, 8 * 512), BF16, es_b1)
    T["attno"] = sbt("s_attno", (128, H * 512), BF16, es_b1)
    T["ln2g"] = sbt("s_ln2g", (128, NCHK * 512), BF16, es_b1)
    T["bd"] = sbt("s_bd", (128, 8 * 16), BF16, es_b1)
    T["nr"] = sbt("s_nr", (H, 512), F32, es_b1)
    T["rr"] = sbt("s_rr", (H, 512), F32, es_b1)
    T["qa"] = sbt("s_qa", (128, 512), F32, es_b1)
    T["qn"] = sbt("s_qn", (128, 512), F32, es_b1)
    T["rhb"] = sbt("s_rhb", (H, 512), BF16, es_b1)
    T["rlb"] = sbt("s_rlb", (H, 512), BF16, es_b1)
    T["rtmp"] = sbt("s_rtmp", (H, 512), F32, es_b1)
    es_b1.close()
    es_pa.close()

    es_c = ExitStack()
    T["wqt2"] = sbt("s_wqt2", (128, NCHK * D), BF16, es_c)
    T["lnf2"] = sbt("s_lnf2", (128, D), F32, es_c)
    T["frac2"] = sbt("s_frac2", (128, D), F32, es_c)
    T["t12"] = sbt("s_t12", (128, D), F32, es_c)
    T["bfbh2"] = sbt("s_bfbh2", (128, D), F32, es_c)
    T["qxtm3"] = sbt("s_qxtm3", (128, D), BF16, es_c)
    T["qxfm3"] = sbt("s_qxfm3", (128, D), BF16, es_c)
    es_c.close()

    ps_p0 = ExitStack()
    T["ps_row"] = pst("p_row", (16, 128), F32, ps_p0)
    T["ps_al"] = pst("p_al", (1, 128), F32, ps_p0)
    T["ps_ab"] = pst("p_ab", (128, 1), F32, ps_p0)
    ps_p0.close()

    ps_a1 = ExitStack()
    T["ps_y"] = pst("p_y", (128, D), F32, ps_a1)
    T["ps_tp"] = pst("p_tp", (128, D), BF16, ps_a1)
    ps_a1.close()

    ps_a2 = ExitStack()
    T["ps_kv"] = pst("p_kv", (128, QF + 512), F32, ps_a2)
    T["ps_tp2"] = pst("p_tp2", (128, QF), BF16, ps_a2)
    T["ps_k"] = pst("p_k", (128, QF), F32, ps_a2)
    T["ps_v"] = pst("p_v", (128, QF), F32, ps_a2)
    ps_a2.close()

    ps_effst = ExitStack()
    T["ps_eff"] = pst("p_eff", (128, 2 * QF), F32, ps_effst)
    ps_effst.close()

    ps_b1 = ExitStack()
    T["ps_q"] = [pst(f"p_q{i}", (128, 512), F32, ps_b1) for i in range(2)]
    T["ps_n"] = pst("p_n", (H, 512), F32, ps_b1)
    T["ps_r"] = pst("p_r", (128, 512), F32, ps_b1)
    T["ps_at"] = pst("p_at", (128, D), F32, ps_b1)
    ps_b1.close()

    ps_c = ExitStack()
    T["ps_y2"] = pst("p_y2", (128, D), F32, ps_c)
    T["ps_tp3"] = pst("p_tp3", (128, D), BF16, ps_c)
    ps_c.close()

    sems = {}
    es_sem = ExitStack()
    for s_ in ("ld", "g", "a", "v", "t", "r"):
        sems[s_] = es_sem.enter_context(nc.semaphore("sem_" + s_))

    col = T["col"]

    def program(cx: Cx):
        _pbody(cx)
        total_g = cx.n["g"]
        if cx.mode == "vector":
            cx.eng.wait_ge(cx.sems["g"], total_g)
        total_ld = cx.n["ld"]
        if cx.mode == "sync":
            cx.eng.wait_ge(cx.sems["ld"], total_ld)

    def _pbody(cx: Cx):
        def vdum():
            cx.op("vector", lambda e: e.tensor_scalar(
                T["vdum"][:, 0:1], T["vdum"][:, 0:1], 1.0, None, ALU.mult))
        ringx = Ring(T["ringx"], "sync")
        ringw = Ring(T["ringw"], "sync")
        ringwb = Ring(T["ringwb"], "gpsimd")

        for nm, dst in [("eye", "eye"), ("e16", "e16"),
                        ("ones1x128b", "o1x128b"), ("onesrowf", "orowf"),
                        ("onescolf", "ocolf"), ("onescolb", "ocolb")]:
            cx.dma("sync", lambda e, nm=nm, dst=dst: e.dma_start(
                out=T[dst][:], in_=dr[nm][:, :]))
        cx.dma("sync", lambda e: e.dma_start(
            out=T["o512b"][:], in_=dr["ones512b"][:, :]), key="ld_consts")
        cx.dma("gpsimd", lambda e: e.dma_start(
            out=T["rowsb"][:, 0:D], in_=dr["brows"][0:1, D:2 * D]))
        cx.dma("gpsimd", lambda e: e.dma_start(
            out=T["rowsb"][:, D:2 * D], in_=dr["brows"][0:1, 2 * D:3 * D]),
            key="g_rows")
        cx.dma("gpsimd", lambda e: e.dma_start(
            out=T["wmapqT"][:], in_=dr["wmapqT"][:, :]))
        cx.dma("gpsimd", lambda e: e.dma_start(
            out=T["wmapkT"][:], in_=dr["wmapkT"][:, :]))
        cx.dma("gpsimd", lambda e: e.dma_start(
            out=T["bq2"][:], in_=dr["bq2"][:, :]))
        cx.dma("gpsimd", lambda e: e.dma_start(
            out=T["bk2"][:], in_=dr["bk2"][:, :]), key="g_consts")

        for i in range(NCHK):
            slot = ringw.load(cx, f"v_aw{i}", f"ld_aw{i}",
                              lambda e, s_, i=i: e.dma_start(
                                  out=s_[:], in_=dr["wffnT"][i * 128:(i + 1) * 128, :]))
            cx.wait("vector", f"ld_aw{i}")
            cx.op("vector", lambda e, s_=slot, i=i: e.tensor_reduce(
                T["wsum"][:, i:i + 1], s_[:], AX.X, ALU.add,
                apply_absolute_value=True), key=f"v_aw{i}")
        vdum()
        cx.op("vector", lambda e: e.tensor_reduce(
            T["wsum"][:, 16:17], T["wsum"][:, 0:16], AX.X, ALU.add))
        vdum()
        cx.op("vector", lambda e: e.tensor_scalar(
            T["whl"][:, 0:1], T["wsum"][:, 16:17], 0.0, None, ALU.add))
        vdum()
        cx.op("vector", lambda e: e.tensor_scalar(
            T["wsum"][:, 17:18], T["whl"][:, 0:1], 0.0, None, ALU.add))
        vdum()
        cx.op("vector", lambda e: e.tensor_tensor(
            T["wsum"][:, 17:18], T["wsum"][:, 16:17], T["wsum"][:, 17:18],
            ALU.subtract))
        vdum()
        cx.op("vector", lambda e: e.tensor_scalar(
            T["whl"][:, 1:2], T["wsum"][:, 17:18], 0.0, None, ALU.add), key="v_awtot")
        cx.wait("tensor", "v_awtot")
        cx.wait("tensor", "ld_consts")
        def _alsum(e):
            e.matmul(T["ps_al"][0:1, 0:1], T["whl"][:, 0:1], T["ocolb"][:, :],
                     start=True, stop=False)
            return e.matmul(T["ps_al"][0:1, 0:1], T["whl"][:, 1:2],
                            T["ocolb"][:, :], start=False, stop=True)
        cx.ops("tensor", _alsum, key="t_alsum")
        cx.wait("scalar", "t_alsum")
        cx.op("scalar", lambda e: e.copy(T["alab"][:, :], T["ps_al"][0:1, 0:1]),
              key="a_alsb")
        cx.wait("vector", "a_alsb")
        cx.op("vector", lambda e: e.tensor_scalar(
            T["ahl"][0:1, 0:1], T["alab"][:, :], 0.0, None, ALU.add))
        vdum()
        cx.op("vector", lambda e: e.tensor_scalar(
            T["wsum"][0:1, 17:18], T["ahl"][0:1, 0:1], 0.0, None, ALU.add))
        vdum()
        cx.op("vector", lambda e: e.tensor_tensor(
            T["wsum"][0:1, 17:18], T["alab"][:, :], T["wsum"][0:1, 17:18],
            ALU.subtract))
        vdum()
        cx.op("vector", lambda e: e.tensor_scalar(
            T["ahl"][0:1, 1:2], T["wsum"][0:1, 17:18], 0.0, None, ALU.add), key="v_ahl")
        cx.wait("tensor", "v_ahl")
        def _albc(e):
            e.matmul(T["ps_ab"][:, :], T["o1x128b"][:, :], T["ahl"][0:1, 0:1],
                     start=True, stop=False)
            return e.matmul(T["ps_ab"][:, :], T["o1x128b"][:, :],
                            T["ahl"][0:1, 1:2], start=False, stop=True)
        cx.ops("tensor", _albc, key="t_albc")
        cx.wait("vector", "t_albc")
        cx.op("vector", lambda e: e.tensor_scalar(
            T["acol"][:, 0:1], T["ps_ab"][:, :], 1.0 / (D * D), EPS_Q,
            ALU.mult, ALU.max))
        vdum()
        cx.op("vector", lambda e: e.tensor_scalar(
            T["acol"][:, 1:2], T["acol"][:, 0:1], 0.5, None, ALU.mult))
        cx.op("vector", lambda e: e.tensor_scalar(
            T["acol"][:, 2:3], T["acol"][:, 0:1], -0.5, None, ALU.mult))
        cx.op("vector", lambda e: e.memset(T["acol"][:, 3:4], 1.0))
        vdum()
        cx.op("vector", lambda e: e.tensor_scalar(
            T["acol"][:, 3:4], T["acol"][:, 3:4], 2.0 ** -30, None, ALU.add))
        vdum()
        cx.op("vector", lambda e: e.tensor_scalar(
            T["acol"][:, 3:4], T["acol"][:, 3:4], 1.0, 2.0 ** 30,
            ALU.subtract, ALU.mult), key="v_alpha")

        cx.wait("tensor", "g_consts")
        def _rows(e):
            e.matmul(T["ps_row"][0:16, 0:64], T["bq2"][:, :], T["wmapqT"][:, :],
                     start=True, stop=True)
            return e.matmul(T["ps_row"][0:16, 64:128], T["bk2"][:, :],
                            T["wmapkT"][:, :], start=True, stop=True)
        cx.ops("tensor", _rows, key="t_rows")
        cx.wait("scalar", "t_rows")
        cx.op("scalar", lambda e: e.copy(T["rows16"][:, :], T["ps_row"][:, :]),
              key="a_rows")
        cx.wait("sync", "a_rows")
        for h in range(H):
            cx.dma("sync", lambda e, h=h: e.dma_start(
                out=T["rowsb"][:, 2 * D + h * DIM:2 * D + (h + 1) * DIM],
                in_=T["rows16"][h:h + 1, 0:64]))
            cx.dma("sync", lambda e, h=h: e.dma_start(
                out=T["rowsb"][:, 2 * D + QF + h * DIM:
                               2 * D + QF + (h + 1) * DIM],
                in_=T["rows16"][h:h + 1, 64:128]),
                key="ld_rows" if h == H - 1 else None)
        cx.dma("sync", lambda e: e.dma_start(
            out=T["lnf"][:, :], in_=dr["bfb_bc"][:, :]), key="ld_bfb1")
        cx.wait("scalar", "ld_bfb1")
        cx.op("scalar", lambda e: e.activation(
            T["bfbh"][:, :], T["lnf"][:, :], AF.Copy, scale=0.5), key="a_bfb1")
        if _stop("p0"):
            return

        for i in range(NCHK):
            slot = ringw.load(cx, f"v_wq{i}", f"ld_wq{i}",
                              lambda e, s_, i=i: e.dma_start(
                                  out=s_[:], in_=dr["wffnT"][i * 128:(i + 1) * 128, :]))
            cx.wait("vector", f"ld_wq{i}")
            if i == 0:
                cx.wait("vector", "v_alpha")
            cx.op("vector", lambda e, s_=slot: e.tensor_scalar(
                T["frac"][:, :], s_[:], T["acol"][:, 2:3], None, ALU.is_lt))
            vdum()
            cx.op("vector", lambda e, s_=slot, i=i: e.scalar_tensor_tensor(
                T["wqt"][:, i * D:(i + 1) * D], s_[:], T["acol"][:, 1:2],
                T["frac"][:, :], ALU.is_gt, ALU.subtract), key=f"v_wq{i}")
            cx.wait("gpsimd", f"v_wq{i}")
            cx.dma("gpsimd", lambda e, i=i: e.dma_start(
                out=wqts[:, i * D:(i + 1) * D], in_=T["wqt"][:, i * D:(i + 1) * D]),
                key=f"g_wqs{i}")
        if _stop("a1q"):
            return

        def ffn_phase(ph, ntiles, src_dram, wq_sb, dst_dram, lnf, frac, t1,
                      psy, pstp, qxtm, qxfm, bfb,
                      first_mm_waits=(), first_a_waits=(),
                      first_v_waits=(), src_store_evt=None):
            for t in range(ntiles):
                xk = f"{ph}x{t}"
                pre = []
                if src_store_evt is not None:
                    pre.append(src_store_evt(t))
                slot = ringx.load(cx, f"{ph}vx1_{t}", xk,
                                  lambda e, s_, t=t: e.dma_start(
                                      out=s_[:], in_=src_dram[t * 128:(t + 1) * 128, :]),
                                  pre_waits=pre)
                cx.wait("scalar", xk)
                if t >= 1:
                    cx.wait("scalar", f"{ph}vqx_{t-1}")
                elif first_a_waits:
                    for w in first_a_waits:
                        cx.wait("scalar", w)
                cx.op("scalar", lambda e, s_=slot: e.activation(
                    lnf[:, :], s_[:], AF.Square, accum_out=col[:, 1:2]))
                cx.op("scalar", lambda e, s_=slot: e.activation(
                    lnf[:, :], s_[:], AF.Identity, accum_out=col[:, 0:1]),
                    key=f"{ph}ast_{t}")
                cx.wait("vector", f"{ph}ast_{t}")
                cx.op("vector", lambda e: e.tensor_scalar(
                    col[:, 2:3], col[:, 0:1], 1.0 / D, None, ALU.mult))
                cx.op("vector", lambda e: e.tensor_scalar(
                    col[:, 3:4], col[:, 1:2], 1.0 / D, None, ALU.mult))
                cx.op("vector", lambda e: e.tensor_tensor(
                    col[:, 4:5], col[:, 2:3], col[:, 2:3], ALU.mult))
                vdum()
                cx.op("vector", lambda e: e.tensor_tensor(
                    col[:, 5:6], col[:, 3:4], col[:, 4:5], ALU.subtract))
                vdum()
                cx.op("vector", lambda e: e.tensor_scalar(
                    col[:, 5:6], col[:, 5:6], EPS_LN, None, ALU.add),
                    key=f"{ph}vvar_{t}")
                cx.wait("scalar", f"{ph}vvar_{t}")
                cx.op("scalar", lambda e: e.activation(
                    col[:, 6:7], col[:, 5:6], AF.Ln))
                cx.op("scalar", lambda e: e.activation(
                    col[:, 7:8], col[:, 6:7], AF.Exp, scale=-0.5),
                    key=f"{ph}ainv_{t}")
                cx.wait("vector", f"{ph}ainv_{t}")
                cx.op("vector", lambda e: e.scalar_tensor_tensor(
                    col[:, 8:9], col[:, 2:3], -1.0, col[:, 7:8],
                    ALU.mult, ALU.mult), key=f"{ph}vB_{t}")
                cx.wait("scalar", f"{ph}vB_{t}")
                cx.op("scalar", lambda e, s_=slot: e.activation(
                    lnf[:, :], s_[:], AF.Identity,
                    bias=col[:, 8:9], scale=col[:, 7:8]), key=f"{ph}aln_{t}")
                cx.wait("vector", f"{ph}aln_{t}")
                if t >= 1:
                    cx.wait("vector", f"{ph}ttp_{t-1}")
                elif first_v_waits:
                    for w in first_v_waits:
                        cx.wait("vector", w)
                cx.op("vector", lambda e: e.tensor_reduce(
                    col[:, 9:10], lnf[:, :], AX.X, ALU.max))
                cx.op("vector", lambda e: e.tensor_reduce(
                    col[:, 13:14], lnf[:, :], AX.X, ALU.min, negate=True))
                vdum()
                cx.op("vector", lambda e: e.tensor_tensor(
                    col[:, 9:10], col[:, 9:10], col[:, 13:14], ALU.max))
                vdum()
                cx.op("vector", lambda e: e.tensor_scalar(
                    col[:, 10:11], col[:, 9:10], EPS_Q, 1.0 / 127.0,
                    ALU.max, ALU.mult))
                vdum()
                cx.op("vector", lambda e: e.reciprocal(
                    col[:, 14:15], col[:, 10:11]))
                vdum()
                cx.op("vector", lambda e: e.tensor_scalar(
                    t1[:, :], lnf[:, :], col[:, 14:15], None, ALU.mult))
                vdum()
                cx.op("vector", lambda e: e.tensor_scalar(
                    frac[:, :], t1[:, :], 1.5 * 2.0 ** 23, 1.5 * 2.0 ** 23,
                    ALU.add, ALU.subtract))
                cx.op("vector", lambda e: e.tensor_scalar(
                    t1[:, :], t1[:, :], 1.5 * 2.0 ** 52, 1.5 * 2.0 ** 52,
                    ALU.add, ALU.subtract))
                vdum()
                cx.op("vector", lambda e: e.tensor_tensor(
                    t1[:, :], t1[:, :], frac[:, :], ALU.subtract))
                vdum()
                cx.op("vector", lambda e: e.scalar_tensor_tensor(
                    qxtm[:, :], t1[:, :], T["acol"][:, 3:4], frac[:, :],
                    ALU.mult, ALU.add))
                cx.op("vector", lambda e: e.tensor_tensor(
                    col[:, 11:12], col[:, 10:11], T["acol"][:, 1:2], ALU.mult),
                    key=f"{ph}vqx_{t}")
                cx.wait("tensor", f"{ph}vqx_{t}")
                if t >= 1:
                    cx.wait("tensor", f"{ph}atp_{t-1}")
                def _tp(e):
                    last = None
                    for i in range(NCHK):
                        last = e.transpose(pstp[:, i * 128:(i + 1) * 128],
                                           qxtm[:, i * 128:(i + 1) * 128],
                                           T["eye"][:, :])
                    return last
                cx.ops("tensor", _tp, key=f"{ph}ttp_{t}")
                cx.wait("scalar", f"{ph}ttp_{t}")
                if t >= 1:
                    cx.wait("scalar", f"{ph}tmm_{t-1}")
                def _atp(e):
                    e.activation(qxfm[:, 0:1024], pstp[:, 0:1024], AF.Copy)
                    return e.activation(qxfm[:, 1024:2048], pstp[:, 1024:2048],
                                        AF.Copy)
                cx.ops("scalar", _atp, key=f"{ph}atp_{t}")
                cx.wait("tensor", f"{ph}atp_{t}")
                if t == 0:
                    for wk_ in first_mm_waits:
                        cx.wait("tensor", wk_)
                if t >= 1:
                    cx.wait("tensor", f"{ph}aev_{t-1}")
                def _mm(e):
                    last = None
                    for i in range(NCHK):
                        for q in range(4):
                            last = e.matmul(
                                psy[:, q * 512:(q + 1) * 512],
                                qxfm[:, i * 128:(i + 1) * 128],
                                wq_sb[:, i * D + q * 512:i * D + (q + 1) * 512],
                                start=(i == 0), stop=(i == NCHK - 1))
                    return last
                cx.ops("tensor", _mm, key=f"{ph}tmm_{t}")
                cx.wait("scalar", f"{ph}tmm_{t}")
                if t >= 1:
                    cx.wait("scalar", f"{ph}vres_{t-1}")
                def _aev(e):
                    last = None
                    for q in range(4):
                        last = e.activation(
                            t1[:, q * 512:(q + 1) * 512],
                            psy[:, q * 512:(q + 1) * 512], AF.Copy,
                            scale=col[:, 11:12])
                    return last
                cx.ops("scalar", _aev, key=f"{ph}aev_{t}")
                cx.wait("vector", f"{ph}aev_{t}")
                if t >= 1:
                    cx.wait("vector", f"{ph}gx1_{t-1}")
                cx.op("vector", lambda e: e.tensor_tensor(
                    frac[:, :], t1[:, :], bfb[:, :], ALU.add),
                    key=f"{ph}vres_{t}")
                vdum()
                cx.op("vector", lambda e, s_=slot: e.tensor_tensor(
                    T["outb"][:, :], frac[:, :], s_[:], ALU.add),
                    key=f"{ph}vx1_{t}")
                cx.wait("gpsimd", f"{ph}vx1_{t}")
                cx.dma("gpsimd", lambda e, t=t: e.dma_start(
                    out=dst_dram[t * 128:(t + 1) * 128, :], in_=T["outb"][:, :]),
                    key=f"{ph}gx1_{t}")

        ffn_phase("a1", TT_ALL, dr["x"], T["wqt"], x1s,
                  T["lnf"], T["frac"], T["t1"], T["ps_y"], T["ps_tp"],
                  T["qxtm"], T["qxfm"], T["bfbh"],
                  first_mm_waits=[f"v_wq{i}" for i in range(NCHK)],
                  first_a_waits=["a_bfb1"])
        if _stop("a1"):
            return

        def eff_build(ph, wsrc, wmap_sb, first_t_waits=()):
            effr = T["eff"][:].rearrange("p (m q) -> p m q", q=QF)
            for hh in range(H):
                slot = ringwb.load(cx, f"{ph}aeff{hh}", f"{ph}ldw{hh}",
                                   lambda e, s_, hh=hh: e.dma_start(
                                       out=s_[:], in_=wsrc[hh * DH:(hh + 1) * DH, :]))
                cx.wait("tensor", f"{ph}ldw{hh}")
                if hh == 0:
                    for wk_ in first_t_waits:
                        cx.wait("tensor", wk_)
                if hh >= 2:
                    cx.wait("tensor", f"{ph}aeff{hh-2}")
                half = (hh % 2) * QF
                def _eff(e, s_=slot, half=half):
                    last = None
                    for m in range(NCHK):
                        last = e.matmul(
                            T["ps_eff"][:, half + m * DIM:half + (m + 1) * DIM],
                            s_[:, m * 128:(m + 1) * 128], wmap_sb[:, :],
                            start=True, stop=True)
                    return last
                cx.ops("tensor", _eff, key=f"{ph}teff{hh}")
                cx.wait("scalar", f"{ph}teff{hh}")
                def _aeff(e, hh=hh, half=half):
                    last = None
                    for b2 in range(2):
                        last = e.activation(
                            effr[:, 8 * b2:8 * b2 + 8, hh * DIM:(hh + 1) * DIM],
                            T["ps_eff"][:, half + b2 * 512:half + (b2 + 1) * 512]
                            .rearrange("p (m d) -> p m d", d=DIM),
                            AF.Copy)
                    return last
                cx.ops("scalar", _aeff, key=f"{ph}aeff{hh}")

        eff_build("ek", dr["wk"], T["wmapkT"],
                  first_t_waits=[f"a1aev_{TT_ALL-1}", f"a1ttp_{TT_ALL-1}",
                                 f"a1gx1_{TT_ALL-1}"])
        if _stop("ek"):
            return

        cx.wait("gpsimd", f"a1vx1_{TT_ALL-1}")
        for i in range(NCHK):
            cx.dma("gpsimd", lambda e, i=i: e.dma_start(
                out=T["wvt"][:, i * D:(i + 1) * D],
                in_=dr["wvT"][i * 128:(i + 1) * 128, :]),
                key=f"g_wv{i}")
        if _stop("wv"):
            return

        cx.wait("vector", f"a1tmm_{TT_ALL-1}")
        cx.wait("vector", f"a1aev_{TT_ALL-1}")
        cx.wait("vector", f"a1atp_{TT_ALL-1}")
        def _vtinit(e):
            last = None
            for i in range(2):
                e.memset(T["vt"][i][:, :], 0.0)
            e.tensor_scalar(T["vdum"][:, 0:1], T["vdum"][:, 0:1], 1.0, None,
                            ALU.mult)
            for i in range(2):
                for h in range(H):
                    last = e.memset(
                        T["vt"][i][:, h * 132 + 128:h * 132 + 129], 1.0)
            return last
        cx.ops("vector", _vtinit, key="v_vtinit")

        def ccol(cc):
            return (cc // 3) * 512 + (cc % 3) * 132

        for t in range(TT_ALL):
            xk = f"a2x{t}"
            slot = ringx.load(cx, f"a2aln_{t}", xk,
                              lambda e, s_, t=t: e.dma_start(
                                  out=s_[:], in_=x1s[t * 128:(t + 1) * 128, :]),
                              pre_waits=[f"a1gx1_{t}"])
            cx.wait("scalar", xk)
            if t >= 1:
                cx.wait("scalar", f"a2vvar_{t-1}")
            else:
                cx.wait("scalar", f"a1gx1_{TT_ALL-1}")
            cx.op("scalar", lambda e, s_=slot: e.activation(
                T["outb"][:, :], s_[:], AF.Square, accum_out=col[:, 1:2]))
            cx.op("scalar", lambda e, s_=slot: e.activation(
                T["outb"][:, :], s_[:], AF.Identity, accum_out=col[:, 0:1]),
                key=f"a2ast_{t}")
            cx.wait("vector", f"a2ast_{t}")
            cx.op("vector", lambda e: e.tensor_scalar(
                col[:, 2:3], col[:, 0:1], 1.0 / D, None, ALU.mult))
            cx.op("vector", lambda e: e.tensor_scalar(
                col[:, 3:4], col[:, 1:2], 1.0 / D, None, ALU.mult))
            cx.op("vector", lambda e: e.tensor_tensor(
                col[:, 4:5], col[:, 2:3], col[:, 2:3], ALU.mult))
            vdum()
            cx.op("vector", lambda e: e.tensor_tensor(
                col[:, 5:6], col[:, 3:4], col[:, 4:5], ALU.subtract))
            vdum()
            cx.op("vector", lambda e: e.tensor_scalar(
                col[:, 5:6], col[:, 5:6], EPS_LN, None, ALU.add),
                key=f"a2vvar_{t}")
            cx.wait("scalar", f"a2vvar_{t}")
            cx.op("scalar", lambda e: e.activation(
                col[:, 6:7], col[:, 5:6], AF.Ln))
            cx.op("scalar", lambda e: e.activation(
                col[:, 7:8], col[:, 6:7], AF.Exp, scale=-0.5), key=f"a2ainv_{t}")
            cx.wait("vector", f"a2ainv_{t}")
            cx.op("vector", lambda e: e.scalar_tensor_tensor(
                col[:, 8:9], col[:, 2:3], -1.0, col[:, 7:8], ALU.mult, ALU.mult),
                key=f"a2vB_{t}")
            cx.wait("scalar", f"a2vB_{t}")
            if t >= 1:
                cx.wait("scalar", f"a2ttp_{t-1}")
            else:
                cx.wait("scalar", f"a1ttp_{TT_ALL-1}")
            cx.op("scalar", lambda e, s_=slot: e.activation(
                T["qxtm2"][:, :], s_[:], AF.Identity,
                bias=col[:, 8:9], scale=col[:, 7:8]), key=f"a2aln_{t}")
            for th in range(2):
                cx.wait("tensor", f"a2aln_{t}" if th == 0 else f"a2atp_{t}_0")
                if t >= 1 or th >= 1:
                    pk = f"a2atp_{t}_{0}" if th == 1 else f"a2atp_{t-1}_1"
                    if th == 1:
                        pass
                    else:
                        cx.wait("tensor", pk)
                def _tp2(e, th=th):
                    last = None
                    for i in range(8 * th, 8 * th + 8):
                        last = e.transpose(
                            T["ps_tp2"][:, (i - 8 * th) * 128:(i - 8 * th + 1) * 128],
                            T["qxtm2"][:, i * 128:(i + 1) * 128],
                            T["eye"][:, :])
                    return last
                cx.ops("tensor", _tp2, key=f"a2ttp_{t}_{th}")
                cx.wait("scalar", f"a2ttp_{t}_{th}")
                if t >= 1 and th == 0:
                    cx.wait("scalar", f"a2tv_{t-1}_1")
                    if (t - 1) < TT_OWN:
                        cx.wait("scalar", f"a2gsp_{t-1}")
                if t == 0 and th == 0:
                    cx.wait("scalar", f"a1tmm_{TT_ALL-1}")
                cx.op("scalar", lambda e, th=th: e.activation(
                    T["qxfm2"][:, 8 * th * 128:(8 * th + 8) * 128],
                    T["ps_tp2"][:, :], AF.Copy), key=f"a2atp_{t}_{th}")
            cx.op("scalar", lambda e: e.copy(col[:, 15:16], col[:, 0:1]),
                  key=f"a2ttp_{t}")
            if t < TT_OWN:
                g4, j = t // 4, t % 4
                cx.wait("gpsimd", f"a2atp_{t}_1")
                cx.dma("gpsimd", lambda e, g4=g4, j=j: e.dma_start(
                    out=ln2s[:].rearrange("p (g i w) -> p g i w", g=4, w=512)[
                        :, g4, :, j * 128:(j + 1) * 128],
                    in_=T["qxfm2"][:].rearrange("p (i c) -> p i c", c=128)),
                    key=f"a2gsp_{t}")
            cx.wait("tensor", f"a2atp_{t}_1")
            if t == 0:
                cx.wait("tensor", f"ekaeff{H-1}")
                cx.wait("tensor", "ld_rows")
            if t >= 2:
                cx.wait("tensor", f"a2vke_{t-2}")
            if t >= 1:
                cx.wait("tensor", f"a2ael_{t-1}")
            def _kmm(e):
                for i in range(NCHK):
                    for q in range(2):
                        e.matmul(T["ps_k"][:, q * 512:(q + 1) * 512],
                                 T["qxfm2"][:, i * 128:(i + 1) * 128],
                                 T["eff"][:, i * QF + q * 512:i * QF + (q + 1) * 512],
                                 start=(i == 0), stop=False)
                last = None
                for q in range(2):
                    last = e.matmul(T["ps_k"][:, q * 512:(q + 1) * 512],
                                    T["o1x128b"][:, :],
                                    T["rowsb"][0:1, 2 * D + QF + q * 512:
                                               2 * D + QF + (q + 1) * 512],
                                    start=False, stop=True)
                return last
            cx.ops("tensor", _kmm, key=f"a2tk_{t}")
            cx.wait("scalar", f"a2tk_{t}")
            if t >= 1:
                cx.wait("scalar", f"a2vke_{t-1}")
            def _ael(e):
                for q in range(2):
                    e.activation(T["ascr"][:, q * 512:(q + 1) * 512],
                                 T["ps_k"][:, q * 512:(q + 1) * 512], AF.Relu)
                    e.activation(T["escr"][:, q * 512:(q + 1) * 512],
                                 T["ps_k"][:, q * 512:(q + 1) * 512],
                                 AF.Relu, scale=-1.0)
                return e.activation(T["escr"][:, :], T["escr"][:, :],
                                    AF.Exp, scale=-1.0)
            cx.ops("scalar", _ael, key=f"a2ael_{t}")
            cx.wait("vector", f"a2ael_{t}")
            if t >= 2:
                cx.wait("vector", f"a2tkv_{t-2}")
            cx.op("vector", lambda e, t=t: e.tensor_tensor(
                T["ke"][t % 2][:, :], T["ascr"][:, :], T["escr"][:, :], ALU.add),
                key=f"a2vke_{t}")
            for vh in range(2):
                if vh == 0:
                    cx.wait("tensor", f"a2ael_{t}")
                    if t == 0:
                        for i in range(NCHK):
                            cx.wait("tensor", f"g_wv{i}")
                        cx.wait("tensor", f"a2ttp_{t}")
                    if t >= 1:
                        cx.wait("tensor", f"a2av_{t-1}_1")
                else:
                    cx.wait("tensor", f"a2av_{t}_0")
                def _vmm(e, vh=vh):
                    for i in range(NCHK):
                        for q in range(2):
                            e.matmul(T["ps_v"][:, q * 512:(q + 1) * 512],
                                     T["qxfm2"][:, i * 128:(i + 1) * 128],
                                     T["wvt"][:, i * D + vh * QF + q * 512:
                                              i * D + vh * QF + (q + 1) * 512],
                                     start=(i == 0), stop=False)
                    last = None
                    for q in range(2):
                        last = e.matmul(T["ps_v"][:, q * 512:(q + 1) * 512],
                                        T["o1x128b"][:, :],
                                        T["rowsb"][0:1, vh * QF + q * 512:
                                                   vh * QF + (q + 1) * 512],
                                        start=False, stop=True)
                    return last
                cx.ops("tensor", _vmm, key=f"a2tv_{t}_{vh}")
                cx.wait("scalar", f"a2tv_{t}_{vh}")
                if t >= 2 and vh == 0:
                    cx.wait("scalar", f"a2tkv_{t-2}")
                def _av(e, t=t, vh=vh):
                    last = None
                    for h2 in range(8):
                        hh = vh * 8 + h2
                        last = e.activation(
                            T["vt"][t % 2][:, hh * 132:hh * 132 + 128],
                            T["ps_v"][:, h2 * 128:(h2 + 1) * 128],
                            AF.Copy)
                    return last
                cx.ops("scalar", _av, key=f"a2av_{t}_{vh}")
            cx.wait("tensor", f"a2vke_{t}")
            cx.wait("tensor", f"a2av_{t}_1")
            if t == 0:
                cx.wait("tensor", "v_vtinit")
            def _kv(e, t=t):
                last = None
                for h in range(H):
                    pb = (h % 2) * 64
                    cc = h // 2
                    last = e.matmul(
                        T["ps_kv"][pb:pb + 64, ccol(cc):ccol(cc) + 129],
                        T["ke"][t % 2][:, h * DIM:(h + 1) * DIM],
                        T["vt"][t % 2][:, h * 132:h * 132 + 129],
                        start=(t == 0), stop=(t == TT_ALL - 1),
                        tile_position=(0, pb),
                        skip_group_check=(pb != 0))
                return last
            cx.ops("tensor", _kv, key=f"a2tkv_{t}")
        if _stop("a2"):
            return

        cx.wait("scalar", f"a2tkv_{TT_ALL-1}")
        def _kvev(e):
            last = None
            for b3 in range(3):
                last = e.activation(T["kv"][:, b3 * 512:(b3 + 1) * 512],
                                    T["ps_kv"][:, b3 * 512:(b3 + 1) * 512],
                                    AF.Copy)
            return last
        cx.ops("scalar", _kvev, key="a_kvev")

        eff_build("eq", dr["wq"], T["wmapqT"],
                  first_t_waits=[f"a2tk_{TT_ALL-1}", "a_kvev"])

        cx.wait("gpsimd", f"a2tkv_{TT_ALL-1}")
        cx.wait("gpsimd", f"eqteff{H-1}")
        for i in range(NCHK):
            cx.dma("gpsimd", lambda e, i=i: e.dma_start(
                out=T["wot"][:, i * D:(i + 1) * D],
                in_=dr["woT"][i * 128:(i + 1) * 128, :]),
                key=f"g_wo{i}")

        cx.wait("vector", "a_kvev")
        cx.wait("vector", f"eqaeff{H-1}")
        cx.op("vector", lambda e: e.memset(T["bd"][:, :], 0.0))
        vdum()
        def _bdc(e):
            last = None
            for g in range(8):
                ks = ccol(g) + 128
                e.tensor_copy(T["bd"][0:64, g * 16 + 2 * g:g * 16 + 2 * g + 1],
                              T["kv"][0:64, ks:ks + 1])
                last = e.tensor_copy(
                    T["bd"][64:128, g * 16 + 2 * g + 1:g * 16 + 2 * g + 2],
                    T["kv"][64:128, ks:ks + 1])
            return last
        cx.ops("vector", _bdc, key="v_bd")
        if _stop("eqwobd"):
            return

        for g4 in range(4):
            lk = f"b1ldg{g4}"
            pre = [f"a2gsp_{4*g4+3}"]
            if g4 == 0:
                pre.append(f"a2tkv_{TT_ALL-1}")
                pre.append(f"eqteff{H-1}")
            if g4 >= 1:
                pre.append(f"b1tq_{g4-1}_7")
            for w in pre:
                cx.wait("sync", w)
            cx.dma("sync", lambda e, g4=g4: e.dma_start(
                out=T["ln2g"][:, :],
                in_=ln2s[:, g4 * NCHK * 512:(g4 + 1) * NCHK * 512]), key=lk)
            for g in range(8):
                cx.wait("tensor", lk)
                if g == 0 and g4 == 0:
                    cx.wait("tensor", f"eqaeff{H-1}")
                    cx.wait("tensor", "ld_rows")
                prev = (g4 * 8 + g) - 2
                if prev >= 0:
                    cx.wait("tensor", f"b1ael_{prev//8}_{prev%8}")
                pq = T["ps_q"][g % 2]
                def _qmm(e, g=g, pq=pq):
                    for i in range(NCHK):
                        e.matmul(pq[:, :],
                                 T["eff"][:, i * QF + g * 128:i * QF + (g + 1) * 128],
                                 T["ln2g"][:, i * 512:(i + 1) * 512],
                                 start=(i == 0), stop=False)
                    return e.matmul(
                        pq[:, :],
                        T["rowsb"][0:1, 2 * D + g * 128:2 * D + (g + 1) * 128],
                        T["o512b"][:, :], start=False, stop=True)
                cx.ops("tensor", _qmm, key=f"b1tq_{g4}_{g}")
                cx.wait("scalar", f"b1tq_{g4}_{g}")
                prev1 = (g4 * 8 + g) - 1
                if prev1 >= 0:
                    cx.wait("scalar", f"b1vqe_{prev1//8}_{prev1%8}")
                cx.op("scalar", lambda e, pq=pq: e.activation(
                    T["qa"][:, :], pq[:, :], AF.Relu))
                cx.op("scalar", lambda e, pq=pq: e.activation(
                    T["qn"][:, :], pq[:, :], AF.Relu, scale=-1.0))
                cx.op("scalar", lambda e: e.activation(
                    T["qn"][:, :], T["qn"][:, :], AF.Exp, scale=-1.0),
                    key=f"b1ael_{g4}_{g}")
                cx.wait("vector", f"b1ael_{g4}_{g}")
                if g == 0 and g4 >= 1:
                    cx.wait("vector", f"b1tat_{g4-1}_3")
                cx.op("vector", lambda e, g=g: e.tensor_tensor(
                    T["qe"][:, g * 512:(g + 1) * 512], T["qa"][:, :],
                    T["qn"][:, :], ALU.add), key=f"b1vqe_{g4}_{g}")
            cx.wait("tensor", f"b1vqe_{g4}_7")
            cx.wait("tensor", "v_bd")
            if g4 >= 1:
                cx.wait("tensor", f"b1vr_{g4-1}")
            def _nmm(e):
                last = None
                for g in range(8):
                    last = e.matmul(T["ps_n"][:, :],
                                    T["bd"][:, g * 16:(g + 1) * 16],
                                    T["qe"][:, g * 512:(g + 1) * 512],
                                    start=(g == 0), stop=(g == 7))
                return last
            cx.ops("tensor", _nmm, key=f"b1tn_{g4}")
            cx.wait("vector", f"b1tn_{g4}")
            cx.op("vector", lambda e: e.tensor_scalar(
                T["nr"][:, :], T["ps_n"][:, :], EPS_A, None, ALU.add))
            vdum()
            cx.op("vector", lambda e: e.reciprocal(T["rr"][:, :], T["nr"][:, :]))
            vdum()
            cx.op("vector", lambda e: e.tensor_scalar(T["rhb"][:, :], T["rr"][:, :], 0.0, None, ALU.add))
            vdum()
            cx.op("vector", lambda e: e.tensor_scalar(T["rtmp"][:, :], T["rhb"][:, :], 0.0, None, ALU.add))
            vdum()
            cx.op("vector", lambda e: e.tensor_tensor(
                T["rtmp"][:, :], T["rr"][:, :], T["rtmp"][:, :], ALU.subtract))
            vdum()
            cx.op("vector", lambda e: e.tensor_scalar(T["rlb"][:, :], T["rtmp"][:, :], 0.0, None, ALU.add),
                  key=f"b1vr_{g4}")
            for g in range(8):
                cx.wait("tensor", f"b1vr_{g4}")
                if g >= 1:
                    cx.wait("tensor", f"b1vqs_{g4}_{g-1}")
                def _rbc(e, g=g):
                    e.matmul(T["ps_r"][:, :], T["e16"][:, g * 128:(g + 1) * 128],
                             T["rhb"][:, :], start=True, stop=False)
                    return e.matmul(T["ps_r"][:, :],
                                    T["e16"][:, g * 128:(g + 1) * 128],
                                    T["rlb"][:, :], start=False, stop=True)
                cx.ops("tensor", _rbc, key=f"b1tr_{g4}_{g}")
                cx.wait("vector", f"b1tr_{g4}_{g}")
                cx.op("vector", lambda e, g=g: e.tensor_tensor(
                    T["qe"][:, g * 512:(g + 1) * 512],
                    T["qe"][:, g * 512:(g + 1) * 512],
                    T["ps_r"][:, :], ALU.mult), key=f"b1vqs_{g4}_{g}")
            for hq in range(4):
                cx.wait("tensor", f"b1vqs_{g4}_{2*hq+1}")
                if hq >= 1:
                    cx.wait("tensor", f"b1aat_{g4}_{hq-1}")
                elif g4 >= 1:
                    cx.wait("tensor", f"b1vx2_{g4-1}_3")
                def _atmm(e, hq=hq):
                    last = None
                    for hh in range(4 * hq, 4 * hq + 4):
                        pb = (hh % 2) * 64
                        cc = hh // 2
                        last = e.matmul(
                            T["ps_at"][:, (hh % 4) * 512:(hh % 4 + 1) * 512],
                            T["kv"][pb:pb + 64, ccol(cc):ccol(cc) + 128],
                            T["qe"][pb:pb + 64, cc * 512:(cc + 1) * 512],
                            start=True, stop=True,
                            tile_position=(pb, 0),
                            skip_group_check=(pb != 0))
                    return last
                cx.ops("tensor", _atmm, key=f"b1tat_{g4}_{hq}")
                cx.wait("scalar", f"b1tat_{g4}_{hq}")
                if hq == 0 and g4 >= 1:
                    cx.wait("scalar", f"b1ty2_{g4-1}_3")
                def _aat(e, hq=hq):
                    last = None
                    for q in range(4):
                        last = e.activation(
                            T["attno"][:, hq * D + q * 512:hq * D + (q + 1) * 512],
                            T["ps_at"][:, q * 512:(q + 1) * 512], AF.Copy)
                    return last
                cx.ops("scalar", _aat, key=f"b1aat_{g4}_{hq}")
            for j in range(4):
                t_own = 4 * g4 + j
                xk = f"b1x{t_own}"
                slot = ringx.load(cx, f"b1vx2_{g4}_{j}", xk,
                                  lambda e, s_, t_own=t_own: e.dma_start(
                                      out=s_[:],
                                      in_=x1s[t_own * 128:(t_own + 1) * 128, :]))
                cx.wait("tensor", f"b1aat_{g4}_3")
                if j >= 1:
                    cx.wait("tensor", f"b1vx2_{g4}_{j-1}")
                def _omm(e, j=j):
                    for c in range(NCHK):
                        for q in range(4):
                            e.matmul(T["ps_at"][:, q * 512:(q + 1) * 512],
                                     T["attno"][:, c * 512 + j * 128:
                                                c * 512 + (j + 1) * 128],
                                     T["wot"][:, c * D + q * 512:c * D + (q + 1) * 512],
                                     start=(c == 0), stop=False)
                    last = None
                    for q in range(4):
                        last = e.matmul(T["ps_at"][:, q * 512:(q + 1) * 512],
                                        T["o1x128b"][:, :],
                                        T["rowsb"][0:1, D + q * 512:
                                                   D + (q + 1) * 512],
                                        start=False, stop=True)
                    return last
                if j == 0 and g4 == 0:
                    for i in range(NCHK):
                        cx.wait("tensor", f"g_wo{i}")
                cx.ops("tensor", _omm, key=f"b1ty2_{g4}_{j}")
                cx.wait("vector", f"b1ty2_{g4}_{j}")
                cx.wait("vector", xk)
                if g4 == 0 and j == 0:
                    cx.wait("vector", f"a2ast_{TT_ALL-1}")
                else:
                    cx.wait("vector", f"b1gx2_{t_own-1}")
                def _vx2(e, s_=slot):
                    last = None
                    for q in range(4):
                        last = e.tensor_tensor(
                            T["outb"][:, q * 512:(q + 1) * 512],
                            T["ps_at"][:, q * 512:(q + 1) * 512],
                            s_[:, q * 512:(q + 1) * 512], ALU.add)
                    return last
                cx.ops("vector", _vx2, key=f"b1vx2_{g4}_{j}")
                cx.wait("gpsimd", f"b1vx2_{g4}_{j}")
                cx.dma("gpsimd", lambda e, t_own=t_own: e.dma_start(
                    out=x2s[t_own * 128:(t_own + 1) * 128, :], in_=T["outb"][:, :]),
                    key=f"b1gx2_{t_own}")
        if _stop("b1"):
            return

        for i in range(4):
            cx.wait("sync", f"b1ty2_3_3")
            cx.dma("sync", lambda e, i=i: e.dma_start(
                out=T["wqt2"][:, i * 4 * D:(i + 1) * 4 * D],
                in_=wqts[:, i * 4 * D:(i + 1) * 4 * D]), key=f"ld_wqt2_{i}")

        cx.wait("sync", "b1ty2_3_3")
        cx.wait("sync", "b1vx2_3_3")
        cx.dma("sync", lambda e: e.dma_start(
            out=T["lnf2"][:, :], in_=dr["bfb_bc"][:, :]), key="ld_bfb2")
        cx.wait("scalar", "ld_bfb2")
        cx.op("scalar", lambda e: e.activation(
            T["bfbh2"][:, :], T["lnf2"][:, :], AF.Copy, scale=0.5),
            key="a_bfb2")
        ffn_phase("c", TT_OWN, x2s, T["wqt2"], y_out,
                  T["lnf2"], T["frac2"], T["t12"], T["ps_y2"], T["ps_tp3"],
                  T["qxtm3"], T["qxfm3"], T["bfbh2"],
                  first_mm_waits=[f"ld_wqt2_{i}" for i in range(4)],
                  first_a_waits=[f"b1gx2_{TT_OWN-1}", "a_bfb2"],
                  first_v_waits=[f"a2ttp_{TT_ALL-1}"],
                  src_store_evt=lambda t: f"b1gx2_{t}")

    plan = Cx("plan", sems)
    program(plan)

    with nc.Block() as blk:
        def mk(engname):
            def body(eng):
                cx = Cx(engname, sems, eng=eng, evt=plan.evt)
                program(cx)
                assert cx.n == plan.n, f"{engname}: {cx.n} vs {plan.n}"
            return body
        blk.sync(mk("sync"))
        blk.gpsimd(mk("gpsimd"))
        blk.scalar(mk("scalar"))
        blk.vector(mk("vector"))
        blk.tensor(mk("tensor"))

    return nc


def _prep_inputs(inputs):
    x = np.asarray(inputs["L0_attn_hidden_states"], dtype=np.float32).reshape(B, S, D)
    wq = np.ascontiguousarray(np.asarray(inputs["L0_attn_wq"], dtype=np.float32))
    wk = np.ascontiguousarray(np.asarray(inputs["L0_attn_wk"], dtype=np.float32))
    wvT = np.ascontiguousarray(np.asarray(inputs["L0_attn_wv"], dtype=np.float32).T)
    woT = np.ascontiguousarray(np.asarray(inputs["L0_attn_wo"], dtype=np.float32).T)
    wffnT = np.ascontiguousarray(np.asarray(inputs["L0_ffn_weight"], dtype=np.float32).T)
    wmapqT = np.ascontiguousarray(np.asarray(inputs["L0_attn_wmap_q"], dtype=np.float32).T)
    wmapkT = np.ascontiguousarray(np.asarray(inputs["L0_attn_wmap_k"], dtype=np.float32).T)
    bq2 = np.ascontiguousarray(
        np.asarray(inputs["L0_attn_bq"], dtype=np.float32).reshape(H, DH).T)
    bk2 = np.ascontiguousarray(
        np.asarray(inputs["L0_attn_bk"], dtype=np.float32).reshape(H, DH).T)
    brows = np.concatenate([
        np.asarray(inputs["L0_ffn_bias"], dtype=np.float32),
        np.asarray(inputs["L0_attn_bv"], dtype=np.float32),
        np.asarray(inputs["L0_attn_bo"], dtype=np.float32)]).reshape(1, 3 * D)
    bfb_bc = np.ascontiguousarray(np.broadcast_to(
        np.asarray(inputs["L0_ffn_bias"], dtype=np.float32).reshape(1, D),
        (128, D)))
    eye = np.eye(128, dtype=BF)
    e16 = np.zeros((H, 8 * 128), dtype=BF)
    for g in range(8):
        for p in range(128):
            e16[2 * g + p // 64, g * 128 + p] = BF(1.0)
    common = dict(
        wffnT=wffnT, wq=wq, wk=wk, wvT=wvT, woT=woT,
        wmapqT=wmapqT, wmapkT=wmapkT, bq2=bq2, bk2=bk2, brows=brows,
        bfb_bc=bfb_bc, eye=eye, e16=e16,
        ones1x128b=np.ones((1, 128), dtype=BF),
        onesrowf=np.ones((1, 128), dtype=np.float32),
        onescolf=np.ones((128, 1), dtype=np.float32),
        onescolb=np.ones((128, 1), dtype=BF),
        ones512b=np.ones((1, 512), dtype=BF),
    )
    in_maps = []
    for c in range(8):
        b, hf = c // 2, c % 2
        own = x[b, hf * 2048:(hf + 1) * 2048]
        other = x[b, (1 - hf) * 2048:(2 - hf) * 2048]
        xi = np.ascontiguousarray(np.concatenate([own, other], axis=0))
        m = dict(common)
        m["x"] = xi
        in_maps.append(m)
    return in_maps


def kernel(**inputs):
    in_maps = _prep_inputs(inputs)
    nc = build_nc()
    res = run_bass_kernel_spmd(nc, in_maps, core_ids=list(range(8)))
    out = np.empty((B, S, D), dtype=np.float32)
    for c in range(8):
        b, hf = c // 2, c % 2
        out[b, hf * 2048:(hf + 1) * 2048] = res.results[c]["y"]
    return out
